# version 01
# baseline (speedup 1.0000x reference)
import numpy as np
from contextlib import ExitStack

import concourse.bass as bass
import concourse.mybir as mybir
from concourse.bass_utils import run_bass_kernel_spmd

F32 = mybir.dt.float32
BF16 = mybir.dt.bfloat16
ALU = mybir.AluOpType
ACT = mybir.ActivationFunctionType
AX = mybir.AxisListType

NCORES = 8
NSEQ = 2
S = 2048
D = 1024
H = 16
HD = 64
HALF = 32
E = 8
NC = 8
NSL = 4
NTC = 16

DEBUG = False
MAX_PHASE = 99
MAX_SEQ = NSEQ
SUB2 = 99
SUB5 = 99
QKOPS = 99


def _build_order():
    order = []
    for sec in range(2):
        for h in range(H):
            order += [sec * D + h * HD + 2 * i for i in range(HALF)]
            order += [sec * D + h * HD + 2 * i + 1 for i in range(HALF)]
    order += list(range(2 * D, 3 * D))
    return np.array(order)


def _build_consts():
    inv = 1.0 / (10000.0 ** (np.arange(HALF, dtype=np.float64) / HALF))
    th = np.outer(inv, np.arange(S, dtype=np.float64))
    c32 = np.cos(th)
    s32 = np.sin(th)
    COS = np.vstack([c32, c32, c32, c32]).astype(np.float32)
    SIN = np.vstack([s32, s32, s32, s32]).astype(np.float32)
    P = np.zeros((128, 128), dtype=np.float32)
    for g in range(2):
        for i in range(HALF):
            e_ = g * 64 + i
            o_ = g * 64 + HALF + i
            P[o_, e_] = -1.0
            P[e_, o_] = 1.0
    OQK = np.zeros((128, 128), dtype=np.float32)
    for k in range(128):
        OQK[k, (k // 64) * 64:(k // 64) * 64 + 64] = 1.0
    IDENT = np.eye(128, dtype=np.float32)
    E8 = np.zeros((8, 8 * 128), dtype=np.float32)
    for e_ in range(8):
        E8[e_, e_ * 128:(e_ + 1) * 128] = 1.0
    return COS, SIN, P, OQK, IDENT, E8


class Sched:
    ENGINES = ("sync", "tensor", "scalar", "vector")

    def __init__(self, nc, ctx):
        self.nc = nc
        self.ctx = ctx
        self.sems = {e: ctx.enter_context(nc.semaphore(f"sem_{e}")) for e in self.ENGINES}
        self.cnt = {e: 0 for e in self.ENGINES}
        self.ops = {e: [] for e in self.ENGINES}
        self.barsem = ctx.enter_context(nc.semaphore("sem_bar"))
        self.barcnt = 0
        self.streams = []

    def dma(self, stream, fn):
        if stream not in self.sems:
            self.sems[stream] = self.ctx.enter_context(self.nc.semaphore(f"sem_{stream}"))
            self.cnt[stream] = 0
            self.streams.append(stream)
        self.cnt[stream] += 16
        sem = self.sems[stream]
        self.ops["sync"].append(lambda En, fn=fn, sem=sem: fn(En).then_inc(sem, 16))
        return (stream, self.cnt[stream])

    def tok(self, eng):
        return (eng, self.cnt[eng])

    def wait(self, eng, tok):
        self.waits(eng, [tok])

    def waits(self, eng, toks):
        best = {}
        for t in toks:
            if t is None:
                continue
            te, v = t
            if te == eng or v <= 0:
                continue
            best[te] = max(best.get(te, 0), v)
        for te, v in best.items():
            sem = self.sems[te]
            self.ops[eng].append(lambda En, sem=sem, v=v: En.wait_ge(sem, v))

    def self_wait(self, eng, tok):
        if tok is None:
            return
        te, v = tok
        assert te == eng
        if v <= 0:
            return
        sem = self.sems[te]
        self.ops[eng].append(lambda En, sem=sem, v=v: En.wait_ge(sem, v))

    def run(self, eng, fn, inc=0):
        if inc:
            self.cnt[eng] += inc
            sem = self.sems[eng]
            self.ops[eng].append(lambda En, fn=fn, sem=sem, inc=inc: fn(En).then_inc(sem, inc))
        else:
            self.ops[eng].append(fn)
        return (eng, self.cnt[eng])

    def barrier(self):
        self.waits("sync", [(st, self.cnt[st]) for st in self.streams])
        for e in self.ENGINES:
            self.ops[e].append(lambda En, s=self.barsem: En.nop().then_inc(s, 1))
        self.barcnt += len(self.ENGINES)
        v = self.barcnt
        for e in self.ENGINES:
            self.ops[e].append(lambda En, s=self.barsem, v=v: En.wait_ge(s, v))


class Mem:
    BASE = 16512
    LIMIT = 224 * 1024

    def __init__(self, nc):
        self.nc = nc
        self.off = self.BASE
        self.peak = self.off
        self.n = 0

    def alloc(self, name, shape, dtype, align=64):
        self.off = (self.off + align - 1) // align * align
        free_elems = int(np.prod(shape[1:])) if len(shape) > 1 else 1
        nbytes = free_elems * (2 if dtype == BF16 else 4)
        t = self.nc.alloc_sbuf_tensor_at(f"{name}_{self.n}", list(shape), dtype, offset=self.off)
        self.n += 1
        self.off += nbytes
        self.peak = max(self.peak, self.off)
        assert self.off <= self.LIMIT, f"sbuf overflow at {name}: {self.off}"
        return t

    def mark(self):
        return self.off

    def reset(self, m):
        self.off = m


def build_program():
    nc = bass.Bass("TRN2", target_bir_lowering=False)

    xT = nc.dram_tensor("xT", [NSEQ, D, S], F32, kind="ExternalInput").ap()
    qkvwT = nc.dram_tensor("qkvwT", [D, 3 * D], F32, kind="ExternalInput").ap()
    qkvbt_d = nc.dram_tensor("qkvbt_d", [128, 16], F32, kind="ExternalInput").ap()
    vb_d = nc.dram_tensor("vb_d", [1, D], F32, kind="ExternalInput").ap()
    outwT = nc.dram_tensor("outwT", [D, D], F32, kind="ExternalInput").ap()
    outbt_d = nc.dram_tensor("outbt_d", [128, 8], F32, kind="ExternalInput").ap()
    gatew_d = nc.dram_tensor("gatew_d", [128, NC, E], F32, kind="ExternalInput").ap()
    gateb = nc.dram_tensor("gateb", [1, E], F32, kind="ExternalInput").ap()
    expw = nc.dram_tensor("expw", [E, D, D], F32, kind="ExternalInput").ap()
    expb = nc.dram_tensor("expb", [E, D], F32, kind="ExternalInput").ap()
    cosd = nc.dram_tensor("cosd", [128, S], F32, kind="ExternalInput").ap()
    sind = nc.dram_tensor("sind", [128, S], F32, kind="ExternalInput").ap()
    pmatd = nc.dram_tensor("pmatd", [128, 128], F32, kind="ExternalInput").ap()
    oqkd = nc.dram_tensor("oqkd", [128, 128], F32, kind="ExternalInput").ap()
    identd = nc.dram_tensor("identd", [128, 128], F32, kind="ExternalInput").ap()
    e8d = nc.dram_tensor("e8d", [8, 8 * 128], F32, kind="ExternalInput").ap()

    yT = nc.dram_tensor("yT", [NSEQ, D, S], F32, kind="ExternalOutput").ap()

    x2scr = nc.dram_tensor("x2scr", [NSEQ, D, S], F32, kind="Internal").ap()
    ewscr = nc.dram_tensor("ewscr", [NC, 128, E, NC, 128], BF16, kind="Internal").ap()

    dbg = {}
    if DEBUG:
        dbg["q"] = nc.dram_tensor("dbg_q", [D, S], BF16, kind="ExternalOutput").ap()
        dbg["k"] = nc.dram_tensor("dbg_k", [D, S], BF16, kind="ExternalOutput").ap()
        dbg["v"] = nc.dram_tensor("dbg_v", [NTC, 128, H * 65], BF16, kind="ExternalOutput").ap()
        dbg["o"] = nc.dram_tensor("dbg_o", [D, S], BF16, kind="ExternalOutput").ap()
        dbg["x2"] = nc.dram_tensor("dbg_x2", [D, S], F32, kind="ExternalOutput").ap()
        dbg["wt"] = nc.dram_tensor("dbg_wt", [8, S], F32, kind="ExternalOutput").ap()
        dbg["r2"] = nc.dram_tensor("dbg_r2", [D, S], F32, kind="ExternalOutput").ap()
        dbg["xe"] = nc.dram_tensor("dbg_xe", [E, 128, NC * 512], BF16, kind="ExternalOutput").ap()
        dbg["lg"] = nc.dram_tensor("dbg_lg", [128, 4 * E], F32, kind="ExternalOutput").ap()
        dbg["x2b"] = nc.dram_tensor("dbg_x2b", [128, NC * 512], BF16, kind="ExternalOutput").ap()
        dbg["scr1"] = nc.dram_tensor("dbg_scr1", [D, S], F32, kind="ExternalOutput").ap()
        dbg["gate"] = nc.dram_tensor("dbg_gate", [128, 64], F32, kind="ExternalOutput").ap()
        dbg["ew0"] = nc.dram_tensor("dbg_ew0", [128, E * NC * 128], BF16, kind="ExternalOutput").ap()

    with ExitStack() as ctx:
        ctx.enter_context(nc.allow_low_precision(
            reason="bf16 matmul operands with fp32 accumulation are sanctioned for this task"))
        sch = Sched(nc, ctx)
        mem = Mem(nc)
        SY, TE, SC, VE = "sync", "tensor", "scalar", "vector"

        PSA = ctx.enter_context(nc.psum_tensor("PSA", [128, 2048], F32))
        PSB = ctx.enter_context(nc.psum_tensor("PSB", [128, 2048], F32))

        pmat = mem.alloc("pmat", [128, 128], F32)
        oqkb = mem.alloc("oqkb", [128, 128], BF16)
        ident = mem.alloc("ident", [128, 128], F32)
        e8b = mem.alloc("e8b", [8, 8 * 128], BF16)
        onesf = mem.alloc("onesf", [128, 128], F32)
        onesb = mem.alloc("onesb", [128, 128], BF16)
        qkvbT = mem.alloc("qkvbT", [128, 16], F32)
        outbT = mem.alloc("outbT", [128, 8], F32)
        gwb = mem.alloc("gwb", [128, NC * E], BF16)
        ebb = mem.alloc("ebb", [8, D], BF16)
        gbr = mem.alloc("gbr", [128, E], F32)
        vbrep = mem.alloc("vbrep", [128, D], BF16)
        epsq = mem.alloc("epsq", [128, 1], F32)
        epsk = mem.alloc("epsk", [128, 1], F32)
        epsln = mem.alloc("epsln", [128, 1], F32)
        junkg = mem.alloc("junkg", [128, 1], F32)

        persist_mark = mem.mark()

        oqkf = mem.alloc("oqkf", [128, 128], F32)
        e8f = mem.alloc("e8f", [8, 8 * 128], F32)
        gwf = mem.alloc("gwf", [128, NC, E], F32)
        ebf = mem.alloc("ebf", [8, D], F32)
        gbrow = mem.alloc("gbrow", [1, E], F32)
        vbrow = mem.alloc("vbrow", [1, D], F32)
        vbrowb = mem.alloc("vbrowb", [1, D], BF16)

        t_l = None
        for apo, api in (
            (pmat[:, :], pmatd[:, :]), (oqkf[:, :], oqkd[:, :]),
            (ident[:, :], identd[:, :]), (e8f[:, :], e8d[:, :]),
            (qkvbT[:, :], qkvbt_d[:, :]),
            (outbT[:, :], outbt_d[:, :]),
            (gwf[:, :, :], gatew_d[:, :, :]),
            (ebf[:, :], expb[:, :]),
            (gbrow[:, :], gateb[:, :]),
            (vbrow[0:1, :], vb_d[:, :]),
        ):
            t_l = sch.dma("p0", lambda En, o=apo, i=api: En.dma_start(out=o, in_=i))
        tok_loads = t_l

        sch.run(VE, lambda En: En.memset(onesf[:, :], 1.0))
        sch.run(VE, lambda En: En.memset(onesb[:, :], 1.0))
        sch.run(VE, lambda En: En.memset(epsq[:, :], 6.4e-23))
        sch.run(VE, lambda En: En.memset(epsk[:, :], 1e-24))
        sch.run(VE, lambda En: En.memset(epsln[:, :], 1e-5))
        sch.wait(VE, tok_loads)
        sch.run(VE, lambda En: En.tensor_copy(oqkb[:, :], oqkf[:, :]))
        sch.run(VE, lambda En: En.tensor_copy(e8b[:, :], e8f[:, :]))
        sch.run(VE, lambda En: En.tensor_copy(gwb[:, :], gwf[:, :, :].rearrange("p c e -> p (c e)")))
        sch.run(VE, lambda En: En.tensor_copy(ebb[:, :], ebf[:, :]))
        tok_casts = sch.run(VE, lambda En: En.tensor_copy(vbrowb[0:1, :], vbrow[0:1, :]), inc=1)

        sch.waits(TE, [tok_loads, tok_casts])
        tg = sch.run(TE, lambda En: En.matmul(
            PSB[:, 1024:1024 + E], onesf[0:1, :], gbrow[0:1, :], start=True, stop=True), inc=1)
        sch.run(TE, lambda En: En.matmul(
            PSB[:, 0:512], onesb[0:1, :], vbrowb[0:1, 0:512], start=True, stop=True))
        tv = sch.run(TE, lambda En: En.matmul(
            PSB[:, 512:1024], onesb[0:1, :], vbrowb[0:1, 512:1024], start=True, stop=True), inc=1)
        sch.waits(SC, [tg])
        sch.run(SC, lambda En: En.activation(gbr[:, :], PSB[:, 1024:1024 + E], ACT.Copy), inc=1)
        sch.waits(VE, [tg, tv])
        sch.run(VE, lambda En: En.tensor_copy(vbrep[:, :], PSB[:, 0:1024]), inc=1)

        sch.barrier()
        mem.reset(persist_mark)

        tok_ewprep_done = None

        for s in range(min(NSEQ, MAX_SEQ if MAX_PHASE >= 2 else 0)):
            seq_mark = mem.mark()
            q_bf = [mem.alloc(f"q{s}", [128, S], BF16) for _ in range(NC)]
            k_bf = [mem.alloc(f"k{s}", [128, S], BF16) for _ in range(NC)]
            v_tm = [mem.alloc(f"v{s}", [128, H * 65], BF16) for _ in range(NTC)]
            qkv_mark = mem.mark()

            cos = mem.alloc("cos", [128, S], F32)
            sin = mem.alloc("sin", [128, S], F32)
            x_bf = [mem.alloc("xbf", [128, S], BF16) for _ in range(NC)]
            xland = mem.alloc("xland", [128, 1024], F32)
            vwland = mem.alloc("vwland", [128, 1024], F32)
            vw_bf = mem.alloc("vw_bf", [128, NC * D], BF16)
            qkw_f = mem.alloc("qkwf", [128, 8, 128], F32)
            qkw_b = [mem.alloc("qkwb", [128, 1024], BF16) for _ in range(2)]
            qraw = [mem.alloc("qraw", [128, 512], F32) for _ in range(2)]
            _tc1 = mem.alloc("tc1", [128, 512], F32)
            _tc2 = mem.alloc("tc2", [128, 512], F32)
            tc1 = [_tc1, _tc1]
            tc2 = [_tc2, _tc2]
            rr = [mem.alloc("rr", [128, 512], F32) for _ in range(2)]
            sqb = [mem.alloc("sqb", [128, 512], BF16) for _ in range(2)]
            stt_ = [mem.alloc("stt", [128, 512], F32) for _ in range(2)]
            rst = [mem.alloc("rst", [128, 512], BF16) for _ in range(2)]

            sch.dma("p0", lambda En: En.dma_start(out=cos[:, :], in_=cosd[:, :]))
            tok_cs = sch.dma("p0", lambda En: En.dma_start(out=sin[:, :], in_=sind[:, :]))

            tok_vw_prev = None
            for dc in range(NC if SUB2 >= 1 else 0):
                sch.wait(SY, tok_vw_prev)
                tvl = sch.dma("wl", lambda En, dc=dc: En.dma_start(
                    out=vwland[:, :], in_=qkvwT[dc * 128:(dc + 1) * 128, 2 * D:3 * D]))
                sch.wait(VE, tvl)
                tok_vw_prev = sch.run(VE, lambda En, dc=dc: En.tensor_copy(
                    vw_bf[:, dc * 1024:(dc + 1) * 1024], vwland[:, :]), inc=1)
            tok_vw = tok_vw_prev

            tok_x_prev = None
            for dc in range(NC):
                for hh in range(2):
                    sch.wait(SY, tok_x_prev)
                    tl = sch.dma("xl", lambda En, dc=dc, hh=hh, s=s: En.dma_start(
                        out=xland[:, :],
                        in_=xT[s, dc * 128:(dc + 1) * 128, hh * 1024:(hh + 1) * 1024]))
                    sch.wait(VE, tl)
                    tok_x_prev = sch.run(VE, lambda En, dc=dc, hh=hh: En.tensor_copy(
                        x_bf[dc][:, hh * 1024:(hh + 1) * 1024], xland[:, :]), inc=1)
            tok_xall = tok_x_prev

            tok_wcast = [None, None]
            tok_wland_free = None
            tok_evac = {}
            tok_rotfree = {}
            tok_sfree = {}
            tok_parity_free = [None, None]
            for fci in range(16 if SUB2 >= 2 else 0):
                wp = fci % 2
                sch.waits(SY, [tok_wcast[wp]])
                sch.wait(SY, tok_wland_free)
                tlw = sch.dma("wl", lambda En, fci=fci: En.dma_start(
                    out=qkw_f[:, :, :],
                    in_=qkvwT[:, fci * 128:(fci + 1) * 128].rearrange(
                        "(c p) f -> c p f", p=128).transpose([1, 0, 2])))
                sch.wait(VE, tlw)
                tok_wcast[wp] = sch.run(VE, lambda En, wp=wp: En.tensor_copy(
                    qkw_b[wp][:, :], qkw_f[:, :, :].rearrange("p c f -> p (c f)")), inc=1)
                tok_wland_free = tok_wcast[wp]
                is_q = fci < 8
                dest = q_bf[fci] if is_q else k_bf[fci - 8]
                for sl in range(NSL):
                    it = fci * NSL + sl
                    pq = it % 2
                    psqk = PSA[:, pq * 512:(pq + 1) * 512]
                    prot = PSA[:, (2 + pq) * 512:(3 + pq) * 512]
                    pss = PSB[:, pq * 512:(pq + 1) * 512]
                    sch.waits(TE, [tok_wcast[wp], tok_xall, tok_evac.get(pq), tok_cs])
                    for dc in range(NC):
                        t_mm = sch.run(TE, lambda En, dc=dc, wp=wp, sl=sl, psqk=psqk: En.matmul(
                            psqk, qkw_b[wp][:, dc * 128:(dc + 1) * 128],
                            x_bf[dc][:, sl * 512:(sl + 1) * 512],
                            start=(dc == 0), stop=(dc == NC - 1)),
                            inc=(1 if dc == NC - 1 else 0))
                    sch.waits(SC, [t_mm, tok_parity_free[pq]])
                    t_qraw = sch.run(SC, lambda En, pq=pq, fci=fci, psqk=psqk: En.activation(
                        qraw[pq][:, :], psqk, ACT.Identity, bias=qkvbT[:, fci:fci + 1]), inc=1)
                    tok_evac[pq] = t_qraw
                    if QKOPS < 2:
                        tok_parity_free[pq] = t_qraw
                        continue
                    sch.waits(TE, [t_qraw, tok_rotfree.get(pq)])
                    t_rot = sch.run(TE, lambda En, pq=pq, prot=prot: En.matmul(
                        prot, pmat[:, :], qraw[pq][:, :], start=True, stop=True), inc=1)
                    if QKOPS < 3:
                        sch.waits(VE, [t_rot])
                        tok_rotfree[pq] = sch.run(VE, lambda En, pq=pq, prot=prot: En.tensor_copy(
                            tc2[pq][:, :], prot), inc=1)
                        tok_parity_free[pq] = tok_rotfree[pq]
                        continue
                    sch.waits(VE, [t_qraw])
                    sch.run(VE, lambda En, pq=pq, sl=sl: En.tensor_tensor(
                        tc1[pq][:, :], qraw[pq][:, :], cos[:, sl * 512:(sl + 1) * 512], op=ALU.mult))
                    sch.waits(VE, [t_rot])
                    t_tc2 = sch.run(VE, lambda En, pq=pq, sl=sl, prot=prot: En.tensor_tensor(
                        tc2[pq][:, :], prot, sin[:, sl * 512:(sl + 1) * 512], op=ALU.mult), inc=1)
                    tok_rotfree[pq] = t_tc2
                    t_rr = sch.run(VE, lambda En, pq=pq: En.tensor_tensor(
                        rr[pq][:, :], tc1[pq][:, :], tc2[pq][:, :], op=ALU.add), inc=1)
                    if QKOPS < 4:
                        tok_parity_free[pq] = t_rr
                        continue
                    sch.waits(SC, [t_rr])
                    t_sq = sch.run(SC, lambda En, pq=pq: En.activation(
                        sqb[pq][:, :], rr[pq][:, :], ACT.Square), inc=1)
                    sch.waits(TE, [t_sq, tok_sfree.get(pq)])
                    t_ss = sch.run(TE, lambda En, pq=pq, pss=pss: En.matmul(
                        pss, oqkb[:, :], sqb[pq][:, :], start=True, stop=True), inc=1)
                    if QKOPS < 5:
                        sch.waits(VE, [t_ss])
                        tok_sfree[pq] = sch.run(VE, lambda En, pq=pq, pss=pss: En.tensor_copy(
                            stt_[pq][:, :], pss), inc=1)
                        tok_parity_free[pq] = tok_sfree[pq]
                        continue
                    sch.waits(SC, [t_ss])
                    t_st = sch.run(SC, lambda En, pq=pq, is_q=is_q, pss=pss: En.activation(
                        stt_[pq][:, :], pss, ACT.Sqrt,
                        scale=(64.0 if is_q else 1.0),
                        bias=(epsq[:, 0:1] if is_q else epsk[:, 0:1])), inc=1)
                    tok_sfree[pq] = t_st
                    sch.waits(VE, [t_st])
                    sch.run(VE, lambda En, pq=pq: En.reciprocal(rst[pq][:, :], stt_[pq][:, :]))
                    sch.run(VE, lambda En: En.memset(junkg[:, 0:1], 0.0))
                    t_fin = sch.run(VE, lambda En, pq=pq, dest=dest, sl=sl: En.tensor_tensor(
                        dest[:, sl * 512:(sl + 1) * 512], rr[pq][:, :], rst[pq][:, :], op=ALU.mult), inc=1)
                    tok_parity_free[pq] = t_fin

            tok_vfree = [None, None]
            for tc in range(NTC if SUB2 >= 3 else 0):
                vp = tc % 2
                psv = PSB[:, vp * 1024:(vp + 1) * 1024]
                sch.run(VE, lambda En, tc=tc: En.memset(
                    v_tm[tc][:, :].rearrange("p (h w) -> p h w", h=H)[:, :, 64:65], 1.0))
                sch.waits(TE, [tok_xall, tok_vw, tok_vfree[vp], tok_sfree.get(0), tok_sfree.get(1)])
                for dc in range(NC):
                    for hf in range(2):
                        t_vmm = sch.run(TE, lambda En, dc=dc, hf=hf, tc=tc, psv=psv: En.matmul(
                            psv[:, hf * 512:(hf + 1) * 512],
                            x_bf[dc][:, tc * 128:(tc + 1) * 128],
                            vw_bf[:, dc * 1024 + hf * 512: dc * 1024 + (hf + 1) * 512],
                            start=(dc == 0), stop=(dc == NC - 1)),
                            inc=(1 if dc == NC - 1 and hf == 1 else 0))
                sch.waits(VE, [t_vmm])
                tok_vfree[vp] = sch.run(VE, lambda En, tc=tc, psv=psv: En.scalar_tensor_tensor(
                    v_tm[tc][:, :].rearrange("p (h w) -> p h w", h=H)[:, :, 0:64],
                    psv.rearrange("p (h w) -> p h w", h=H),
                    0.0, vbrep[:, :].rearrange("p (h w) -> p h w", h=H),
                    op0=ALU.add, op1=ALU.add), inc=1)

            if DEBUG and s == 0:
                sch.barrier()
                for dc in range(NC):
                    sch.dma("dbg", lambda En, dc=dc: En.dma_start(
                        out=dbg["q"][dc * 128:(dc + 1) * 128, :], in_=q_bf[dc][:, :]))
                    sch.dma("dbg", lambda En, dc=dc: En.dma_start(
                        out=dbg["k"][dc * 128:(dc + 1) * 128, :], in_=k_bf[dc][:, :]))
                for tc in range(NTC):
                    sch.dma("dbg", lambda En, tc=tc: En.dma_start(
                        out=dbg["v"][tc, :, :], in_=v_tm[tc][:, :]))

            sch.barrier()
            mem.reset(qkv_mark)
            if MAX_PHASE < 3:
                continue

            o_pr = [mem.alloc(f"o{s}", [128, S], BF16) for _ in range(NC)]
            opr_mark = mem.mark()
            p_bf = [mem.alloc("pbf", [128, 1024], BF16) for _ in range(2)]
            o_scr = [mem.alloc("oscr", [128, S], BF16) for _ in range(2)]
            r1b = [mem.alloc("r1b", [128, 512], BF16) for _ in range(2)]
            rep_s = [mem.alloc("reps", [128, 512], BF16) for _ in range(2)]
            if s == 0:
                ewland = mem.alloc("ewland", [128, 4, 1024], F32)
                ewb2 = mem.alloc("ewb2", [128, 4096], BF16)

            tok_exp = {}
            tok_pv = {}
            tok_scores_free = None
            tok_pv_free = None
            tok_osc_use = [None, None]
            tok_ewstep = None
            osp = 0
            for h in range(H):
                hc, r0 = h // 2, (h % 2) * 64
                for c in range(NTC):
                    for half in range(2):
                        p2 = (c * 2 + half) % 2
                        psc = PSA[:, p2 * 1024:(p2 + 1) * 1024]
                        sch.waits(TE, [tok_exp.get(p2), tok_scores_free, tok_pv_free])
                        for j in range(2):
                            t_sc = sch.run(TE, lambda En, hc=hc, r0=r0, c=c, half=half, j=j, psc=psc: En.matmul(
                                psc[:, j * 512:(j + 1) * 512],
                                k_bf[hc][r0:r0 + 64, c * 128:(c + 1) * 128],
                                q_bf[hc][r0:r0 + 64, half * 1024 + j * 512: half * 1024 + (j + 1) * 512],
                                start=True, stop=True), inc=(1 if j == 1 else 0))
                        sch.waits(SC, [t_sc, tok_pv.get(p2)])
                        tok_exp[p2] = sch.run(SC, lambda En, p2=p2, psc=psc: En.activation(
                            p_bf[p2][:, :], psc, ACT.Exp), inc=1)
                        sch.waits(TE, [tok_exp[p2]])
                        for j in range(2):
                            sl = half * 2 + j
                            tok_pv[p2] = sch.run(TE, lambda En, h=h, c=c, p2=p2, j=j, sl=sl: En.matmul(
                                PSB[0:65, sl * 512:(sl + 1) * 512],
                                v_tm[c][:, h * 65:(h + 1) * 65],
                                p_bf[p2][:, j * 512:(j + 1) * 512],
                                start=(c == 0), stop=(c == NTC - 1)),
                                inc=(1 if j == 1 else 0))
                t_evs = []
                t_repcopy_prev = None
                for sl in range(NSL):
                    slp = sl % 2
                    sch.waits(VE, [tok_pv[0], tok_pv[1]])
                    t_r1 = sch.run(VE, lambda En, sl=sl, slp=slp: En.reciprocal(
                        r1b[slp][64:65, :], PSB[64:65, sl * 512:(sl + 1) * 512]), inc=1)
                    sch.waits(TE, [t_r1, tok_exp[0], tok_exp[1], t_repcopy_prev])
                    t_rep = sch.run(TE, lambda En, slp=slp: En.matmul(
                        PSA[0:64, 0:512], onesb[64:65, 0:64], r1b[slp][64:65, :],
                        start=True, stop=True), inc=1)
                    sch.waits(VE, [t_rep])
                    sch.run(VE, lambda En, slp=slp: En.tensor_copy(rep_s[slp][0:64, :], PSA[0:64, 0:512]))
                    if h % 2 == 0:
                        t_ev = sch.run(VE, lambda En, hc=hc, sl=sl, slp=slp: En.tensor_tensor(
                            o_pr[hc][0:64, sl * 512:(sl + 1) * 512],
                            PSB[0:64, sl * 512:(sl + 1) * 512], rep_s[slp][0:64, :], op=ALU.mult), inc=1)
                    else:
                        sch.waits(VE, [tok_osc_use[osp]])
                        t_ev = sch.run(VE, lambda En, sl=sl, slp=slp, osp=osp: En.tensor_tensor(
                            o_scr[osp][0:64, sl * 512:(sl + 1) * 512],
                            PSB[0:64, sl * 512:(sl + 1) * 512], rep_s[slp][0:64, :], op=ALU.mult), inc=1)
                    t_repcopy_prev = t_ev
                    t_evs.append(t_ev)
                tok_scores_free = t_evs[-1]
                tok_pv_free = t_evs[-1]
                if h % 2 == 1:
                    sch.waits(SY, t_evs)
                    tok_osc_use[osp] = sch.dma("omv", lambda En, hc=hc, osp=osp: En.dma_start(
                        out=o_pr[hc][64:128, :], in_=o_scr[osp][0:64, :]))
                    osp ^= 1
                if s == 0:
                    e_, dh = h // 2, h % 2
                    sch.waits(SY, [tok_ewstep])
                    t_lde = sch.dma("ewp", lambda En, e_=e_, dh=dh: En.dma_start(
                        out=ewland[:, :, :],
                        in_=expw[e_, dh * 512:(dh + 1) * 512, :].rearrange(
                            "(c p) f -> c p f", p=128).transpose([1, 0, 2])))
                    sch.wait(VE, t_lde)
                    t_ce = sch.run(VE, lambda En: En.tensor_copy(
                        ewb2[:, :], ewland[:, :, :].rearrange("p c f -> p (c f)")), inc=1)
                    sch.wait(SY, t_ce)
                    for cc in range(4):
                        tok_ewstep = sch.dma("ewp", lambda En, e_=e_, dh=dh, cc=cc: En.dma_start(
                            out=ewscr[:, :, e_, dh * 4 + cc, :].transpose([1, 0, 2]),
                            in_=ewb2[:, :].rearrange("p (c f w) -> p c f w", c=4, f=8)[:, cc]))
            if s == 0:
                tok_ewprep_done = tok_ewstep

            if DEBUG and s == 0:
                sch.barrier()
                for dc in range(NC):
                    sch.dma("dbg", lambda En, dc=dc: En.dma_start(
                        out=dbg["o"][dc * 128:(dc + 1) * 128, :], in_=o_pr[dc][:, :]))

            sch.barrier()
            mem.reset(opr_mark)
            if MAX_PHASE < 4:
                continue

            owland = mem.alloc("owland", [128, 1024], F32)
            outw_bf = mem.alloc("outw_bf", [128, NC * D], BF16)
            xres = [mem.alloc("xres", [128, 512], F32) for _ in range(2)]
            r_sl = [mem.alloc("rsl", [128, 512], F32) for _ in range(NC)]
            sq_t = [mem.alloc("sqt", [128, 512], BF16) for _ in range(2)]
            m_t = mem.alloc("mt", [128, 512], F32)
            msq = mem.alloc("msq", [128, 512], F32)
            var_t = mem.alloc("vart", [128, 512], F32)
            srt = mem.alloc("srt", [128, 512], F32)
            rstd_t = mem.alloc("rstdt", [128, 512], F32)
            bb_t = mem.alloc("bbt", [128, 512], F32)
            t1_t = [mem.alloc("t1t", [128, 512], F32) for _ in range(2)]
            x2_t = [mem.alloc("x2t", [128, 512], F32) for _ in range(2)]

            tok_ow_prev = None
            for dc in range(NC):
                sch.wait(SY, tok_ow_prev)
                twl = sch.dma("wl", lambda En, dc=dc: En.dma_start(
                    out=owland[:, :], in_=outwT[dc * 128:(dc + 1) * 128, :]))
                sch.wait(VE, twl)
                tok_ow_prev = sch.run(VE, lambda En, dc=dc: En.tensor_copy(
                    outw_bf[:, dc * 1024:(dc + 1) * 1024], owland[:, :]), inc=1)
            tok_ow = tok_ow_prev

            tok_pso_free = [None, None]
            tok_xres_free = [None, None]
            tok_lnfree = None
            tok_x2st = [None, None]
            t_x2_last = None
            t_lnh = {}
            ln_i = 0
            for sl in range(NSL):
                t_ln = None
                for fc in range(NC):
                    pf = fc % 2
                    pso = PSA[:, pf * 512:(pf + 1) * 512]
                    sch.wait(SY, tok_xres_free[pf])
                    t_xr = sch.dma("xl", lambda En, fc=fc, sl=sl, pf=pf, s=s: En.dma_start(
                        out=xres[pf][:, :],
                        in_=xT[s, fc * 128:(fc + 1) * 128, sl * 512:(sl + 1) * 512]))
                    sch.waits(TE, [tok_pso_free[pf], tok_ow])
                    for dc in range(NC):
                        t_om = sch.run(TE, lambda En, dc=dc, fc=fc, sl=sl, pso=pso: En.matmul(
                            pso, outw_bf[:, dc * 1024 + fc * 128: dc * 1024 + (fc + 1) * 128],
                            o_pr[dc][:, sl * 512:(sl + 1) * 512],
                            start=(dc == 0), stop=(dc == NC - 1)),
                            inc=(1 if dc == NC - 1 else 0))
                    sch.waits(VE, [t_om, t_xr])
                    t_r = sch.run(VE, lambda En, fc=fc, pf=pf, pso=pso: En.scalar_tensor_tensor(
                        r_sl[fc][:, :], pso, outbT[:, fc:fc + 1], xres[pf][:, :],
                        op0=ALU.add, op1=ALU.add), inc=1)
                    tok_pso_free[pf] = t_r
                    tok_xres_free[pf] = t_r
                    sch.waits(SC, [t_r, t_lnh.get(ln_i - 2)])
                    t_sq = sch.run(SC, lambda En, fc=fc: En.activation(
                        sq_t[fc % 2][:, :], r_sl[fc][:, :], ACT.Square), inc=1)
                    sch.waits(TE, [t_r, t_sq, tok_lnfree])
                    sch.run(TE, lambda En, fc=fc: En.matmul(
                        PSB[:, 0:512], onesf[:, :], r_sl[fc][:, :],
                        start=(fc == 0), stop=(fc == NC - 1)))
                    t_ln = sch.run(TE, lambda En, fc=fc: En.matmul(
                        PSB[:, 512:1024], onesb[:, :], sq_t[fc % 2][:, :],
                        start=(fc == 0), stop=(fc == NC - 1)), inc=1)
                    t_lnh[ln_i] = t_ln
                    ln_i += 1
                sch.waits(VE, [t_ln])
                t_m = sch.run(VE, lambda En: En.tensor_scalar(
                    m_t[:, :], PSB[:, 0:512], 1.0 / D, None, op0=ALU.mult), inc=1)
                sch.waits(SC, [t_m])
                t_msq = sch.run(SC, lambda En: En.activation(msq[:, :], m_t[:, :], ACT.Square), inc=1)
                sch.waits(VE, [t_msq])
                t_var = sch.run(VE, lambda En: En.scalar_tensor_tensor(
                    var_t[:, :], PSB[:, 512:1024], 1.0 / D, msq[:, :],
                    op0=ALU.mult, op1=ALU.subtract), inc=1)
                tok_lnfree = t_var
                sch.waits(SC, [t_var])
                t_srt = sch.run(SC, lambda En: En.activation(
                    srt[:, :], var_t[:, :], ACT.Sqrt, bias=epsln[:, 0:1]), inc=1)
                sch.waits(VE, [t_srt])
                sch.run(VE, lambda En: En.reciprocal(rstd_t[:, :], srt[:, :]))
                sch.run(VE, lambda En: En.memset(junkg[:, 0:1], 0.0))
                sch.run(VE, lambda En: En.tensor_tensor(bb_t[:, :], m_t[:, :], rstd_t[:, :], op=ALU.mult))
                for fc in range(NC):
                    pf = fc % 2
                    sch.wait(VE, tok_x2st[pf])
                    sch.run(VE, lambda En, fc=fc, pf=pf: En.tensor_tensor(
                        t1_t[pf][:, :], r_sl[fc][:, :], rstd_t[:, :], op=ALU.mult))
                    t_x2 = sch.run(VE, lambda En, fc=fc, pf=pf: En.tensor_tensor(
                        x2_t[pf][:, :], t1_t[pf][:, :], bb_t[:, :], op=ALU.subtract), inc=1)
                    sch.waits(SY, [t_x2])
                    tok_x2st[pf] = sch.dma("x2s", lambda En, fc=fc, sl=sl, pf=pf, s=s: En.dma_start(
                        out=x2scr[s, fc * 128:(fc + 1) * 128, sl * 512:(sl + 1) * 512],
                        in_=x2_t[pf][:, :]))
                    t_x2_last = tok_x2st[pf]

            if DEBUG and s == 0:
                sch.waits(SY, [t_x2_last])
                sch.dma("dbg", lambda En: En.dma_start(out=dbg["x2"][:, :], in_=x2scr[0][:, :]))

            sch.barrier()
            mem.reset(seq_mark)
            if MAX_PHASE < 5:
                continue

            x2sl = [mem.alloc("x2sl", [128, NC, 512], F32) for _ in range(2)]
            x2bf = [mem.alloc("x2bf", [128, NC * 512], BF16) for _ in range(2)]
            x_e = [mem.alloc("xe", [128, NC * 512], BF16) for _ in range(E)]
            ews = [mem.alloc("ews", [128, E * NC * 128], BF16) for _ in range(2)]
            wrep = [mem.alloc("wrep", [128, 512], BF16) for _ in range(2)]
            wTb = mem.alloc("wTb", [8, S], BF16)
            lg = mem.alloc("lg", [128, E], F32)
            lg2 = mem.alloc("lg2", [128, E], F32)
            nmx = mem.alloc("nmx", [128, 1], F32)
            el = mem.alloc("el", [128, E], F32)
            nss = mem.alloc("nss", [128, 1], F32)
            sgn = mem.alloc("sgn", [128, E], F32)
            eu = [mem.alloc("eu", [128, E], F32) for _ in range(2)]
            wuTb = mem.alloc("wuTb", [8, S], BF16)
            rws = mem.alloc("rws", [128, 512], F32)
            gsnap = mem.alloc("gsnap", [128, 64], F32)
            r2_t = [mem.alloc("r2t", [128, 512], F32) for _ in range(NC)]
            sq2_t = [mem.alloc("sq2t", [128, 512], BF16) for _ in range(2)]
            m_t2 = mem.alloc("mt2", [128, 512], F32)
            msq2 = mem.alloc("msq2", [128, 512], F32)
            var2 = mem.alloc("var2", [128, 512], F32)
            srt2 = mem.alloc("srt2", [128, 512], F32)
            rstd2 = mem.alloc("rstd2", [128, 512], F32)
            bb2 = mem.alloc("bb2", [128, 512], F32)
            t1_t2 = [mem.alloc("t1t2", [128, 512], F32) for _ in range(2)]
            y_t = [mem.alloc("yt", [128, 512], F32) for _ in range(2)]

            tok_x2use = [None, None]
            tok_ewuse = [None, None]
            t_ln2h = {}
            ln2_i = 0
            tok_gpsfree = None
            tok_wtr_free = None
            tok_wrepmm_free = None
            tok_emm_free = {}
            tok_ln2free = None
            tok_yst = [None, None]
            if DEBUG and s == 0:
                sch.waits(SY, [t_x2_last, tok_ewprep_done])
                sch.dma("dbg", lambda En: En.dma_start(out=dbg["scr1"][:, :], in_=x2scr[0][:, :]))
                t_dew = sch.dma("dbg", lambda En: En.dma_start(
                    out=dbg["ew0"][:, :], in_=ewscr[0].rearrange("p e c w -> p (e c w)")))
                sch.waits(SY, [t_dew])
            for sl in range(NSL):
                pp = sl % 2
                if sl == 0:
                    sch.waits(SY, [t_x2_last])
                sch.waits(SY, [tok_x2use[pp]])
                t_x2l = sch.dma("x2l", lambda En, sl=sl, pp=pp, s=s: En.dma_start(
                    out=x2sl[pp][:, :, :],
                    in_=x2scr[s][:, sl * 512:(sl + 1) * 512].rearrange(
                        "(c p) f -> c p f", p=128).transpose([1, 0, 2])))
                sch.wait(VE, t_x2l)
                t_x2b = sch.run(VE, lambda En, pp=pp: En.tensor_copy(
                    x2bf[pp][:, :], x2sl[pp][:, :, :].rearrange("p c f -> p (c f)")), inc=1)
                if SUB5 < 1:
                    tok_x2use[pp] = t_x2b
                    continue
                for tcl in range(4):
                    sch.waits(TE, [t_x2b, tok_gpsfree])
                    for dc in range(NC):
                        t_g = sch.run(TE, lambda En, dc=dc, tcl=tcl, pp=pp: En.matmul(
                            PSB[:, 1024:1024 + E],
                            x2bf[pp][:, dc * 512 + tcl * 128: dc * 512 + (tcl + 1) * 128],
                            gwb[:, dc * E:(dc + 1) * E],
                            start=(dc == 0), stop=(dc == NC - 1)),
                            inc=(1 if dc == NC - 1 else 0))
                    sch.waits(SC, [t_g])
                    t_lgs = sch.run(SC, lambda En: En.activation(
                        lg[:, :], PSB[:, 1024:1024 + E], ACT.Copy), inc=1)
                    if DEBUG and s == 0 and sl == 0:
                        sch.waits(SY, [t_lgs])
                        t_dmad = sch.dma("dbg", lambda En, tcl=tcl: En.dma_start(
                            out=dbg["lg"][:, tcl * E:(tcl + 1) * E], in_=lg[:, :]))
                        sch.wait(SC, t_dmad)
                        sch.wait(VE, t_dmad)
                    sch.waits(VE, [t_lgs])
                    sch.run(VE, lambda En: En.tensor_tensor(
                        lg2[:, :], lg[:, :], gbr[:, :], op=ALU.add))
                    if DEBUG and s == 0 and sl == 0 and tcl == 0:
                        sch.dma("dbg", lambda En, pp=pp: En.dma_start(
                            out=dbg["x2b"][:, :], in_=x2bf[pp][:, :]))
                    sch.run(VE, lambda En: En.memset(junkg[:, 0:1], 0.0))
                    t_lg = sch.run(VE, lambda En: En.reduce_max(
                        nmx[:, :], lg2[:, :], axis=AX.X, negate=True), inc=1)
                    tok_gpsfree = t_lgs
                    sch.waits(SC, [t_lg])
                    t_el = sch.run(SC, lambda En: En.activation(
                        el[:, :], lg2[:, :], ACT.Exp, bias=nmx[:, 0:1]), inc=1)
                    sch.waits(VE, [t_el])
                    t_nss = sch.run(VE, lambda En: En.reduce_sum(
                        nss[:, :], el[:, :], axis=AX.X, negate=True), inc=1)
                    sch.waits(SC, [t_nss])
                    t_sgn = sch.run(SC, lambda En: En.activation(
                        sgn[:, :], el[:, :], ACT.Sign, scale=10.0, bias=nss[:, 0:1]), inc=1)
                    sch.waits(VE, [t_sgn])
                    t_eu = sch.run(VE, lambda En, tcl=tcl: En.scalar_tensor_tensor(
                        eu[tcl % 2][:, :], sgn[:, :], 1.0, el[:, :],
                        op0=ALU.add, op1=ALU.min), inc=1)
                    if DEBUG and s == 0 and sl == 0 and tcl == 0:
                        sch.run(VE, lambda En: En.tensor_copy(gsnap[:, 0:8], lg2[:, :]))
                        sch.run(VE, lambda En: En.tensor_copy(gsnap[:, 8:16], el[:, :]))
                        sch.run(VE, lambda En: En.tensor_copy(gsnap[:, 16:24], sgn[:, :]))
                        sch.run(VE, lambda En: En.tensor_copy(gsnap[:, 24:32], eu[0][:, :]))
                        t_gsn = sch.run(VE, lambda En: En.tensor_copy(gsnap[:, 48:49], nss[:, :]), inc=1)
                        sch.waits(SY, [t_gsn])
                        t_gsd = sch.dma("dbg", lambda En: En.dma_start(
                            out=dbg["gate"][:, :], in_=gsnap[:, :]))
                        sch.wait(VE, t_gsd)
                        sch.wait(SC, t_gsd)
                        sch.wait(TE, t_gsd)
                    sch.waits(TE, [t_eu, tok_wtr_free, tok_wrepmm_free])
                    t_tr = sch.run(TE, lambda En, tcl=tcl: En.transpose(
                        PSB[0:8, 1536:1664], eu[tcl % 2][:, :], ident[:, :]), inc=1)
                    sch.waits(SC, [t_tr])
                    t_wt = sch.run(SC, lambda En, sl=sl, tcl=tcl: En.activation(
                        wuTb[0:8, (sl * 4 + tcl) * 128:(sl * 4 + tcl + 1) * 128],
                        PSB[0:8, 1536:1664], ACT.Copy), inc=1)
                    tok_wtr_free = t_wt
                sch.waits(TE, [tok_wtr_free, tok_wrepmm_free])
                t_wsmm = sch.run(TE, lambda En, sl=sl: En.matmul(
                    PSB[:, 1536:2048], onesb[0:8, 0:128],
                    wuTb[0:8, sl * 512:(sl + 1) * 512], start=True, stop=True), inc=1)
                sch.waits(VE, [t_wsmm])
                t_rws = sch.run(VE, lambda En: En.reciprocal(rws[:, :], PSB[:, 1536:2048]), inc=1)
                tok_wrepmm_free = t_rws
                t_wtb = sch.run(VE, lambda En, sl=sl: En.tensor_tensor(
                    wTb[0:8, sl * 512:(sl + 1) * 512], wuTb[0:8, sl * 512:(sl + 1) * 512],
                    rws[0:8, :], op=ALU.mult), inc=1)
                if SUB5 < 2:
                    tok_x2use[pp] = t_x2b
                    continue
                tok_xe_all = None
                t_wrc_last = None
                for e_ in range(E):
                    pe_ = e_ % 2
                    sch.waits(TE, [t_wtb, tok_wtr_free, t_wrc_last])
                    if e_ == E - 1:
                        pass
                    t_wr = sch.run(TE, lambda En, e_=e_, sl=sl: En.matmul(
                        PSB[:, 1536:2048], e8b[0:8, e_ * 128:(e_ + 1) * 128],
                        wTb[0:8, sl * 512:(sl + 1) * 512], start=True, stop=True), inc=1)
                    tok_wrepmm_free = t_wr
                    sch.waits(VE, [t_wr])
                    t_wrc = sch.run(VE, lambda En, pe_=pe_: En.tensor_copy(
                        wrep[pe_][:, :], PSB[:, 1536:2048]), inc=1)
                    t_wrc_last = t_wrc
                    for dc in range(NC):
                        tok_xe_all = sch.run(VE, lambda En, e_=e_, dc=dc, pp=pp, pe_=pe_: En.tensor_tensor(
                            x_e[e_][:, dc * 512:(dc + 1) * 512],
                            x2bf[pp][:, dc * 512:(dc + 1) * 512],
                            wrep[pe_][:, :], op=ALU.mult),
                            inc=(1 if dc == NC - 1 else 0))
                tok_wtr_free = t_wrc_last
                if SUB5 < 3:
                    tok_x2use[pp] = t_x2b
                    continue
                if DEBUG and s == 0:
                    sch.waits(SY, [t_wtb, tok_xe_all])
                    sch.dma("dbg", lambda En, sl=sl: En.dma_start(
                        out=dbg["wt"][:, sl * 512:(sl + 1) * 512].bitcast(BF16)[:, 0:512],
                        in_=wTb[0:8, sl * 512:(sl + 1) * 512]))
                    if sl == 0:
                        for e_ in range(E):
                            sch.dma("dbg", lambda En, e_=e_: En.dma_start(
                                out=dbg["xe"][e_, :, :], in_=x_e[e_][:, :]))
                t_ln2 = None
                for fcg in range(2):
                    for fc4 in range(4):
                        fc = fcg * 4 + fc4
                        ep = fc % 2
                        if s == 0 and sl == 0 and fc < 2:
                            sch.waits(SY, [tok_ewprep_done])
                        sch.waits(SY, [tok_ewuse[ep]])
                        t_ewl = sch.dma("ewl", lambda En, fc=fc, ep=ep: En.dma_start(
                            out=ews[ep][:, :], in_=ewscr[fc].rearrange("p e c w -> p (e c w)")))
                        pse = PSA[:, fc4 * 512:(fc4 + 1) * 512]
                        sch.waits(TE, [t_ewl, tok_xe_all, tok_emm_free.get(fc4)])
                        sch.run(TE, lambda En, fc=fc, sl=sl, pse=pse: En.matmul(
                            pse, ebb[0:8, fc * 128:(fc + 1) * 128],
                            wTb[0:8, sl * 512:(sl + 1) * 512], start=True, stop=False))
                        for e_ in range(E):
                            for dc in range(NC):
                                t_emm = sch.run(TE, lambda En, e_=e_, dc=dc, ep=ep, pse=pse: En.matmul(
                                    pse, ews[ep][:, (e_ * NC + dc) * 128:(e_ * NC + dc + 1) * 128],
                                    x_e[e_][:, dc * 512:(dc + 1) * 512],
                                    start=False, stop=(e_ == E - 1 and dc == NC - 1)),
                                    inc=(1 if e_ == E - 1 and dc == NC - 1 else 0))
                        tok_ewuse[ep] = t_emm
                        sch.waits(VE, [t_emm])
                        t_r2 = sch.run(VE, lambda En, fc=fc, pp=pp, pse=pse: En.scalar_tensor_tensor(
                            r2_t[fc][:, :], pse, 0.0, x2sl[pp][:, fc, :],
                            op0=ALU.add, op1=ALU.add), inc=1)
                        tok_emm_free[fc4] = t_r2
                        tok_x2use[pp] = t_r2
                        if DEBUG and s == 0:
                            sch.waits(SY, [t_r2])
                            sch.dma("dbg", lambda En, fc=fc, sl=sl: En.dma_start(
                                out=dbg["r2"][fc * 128:(fc + 1) * 128, sl * 512:(sl + 1) * 512],
                                in_=r2_t[fc][:, :]))
                        sch.waits(SC, [t_r2, t_ln2h.get(ln2_i - 2)])
                        t_sq2 = sch.run(SC, lambda En, fc=fc: En.activation(
                            sq2_t[fc % 2][:, :], r2_t[fc][:, :], ACT.Square), inc=1)
                        sch.waits(TE, [t_r2, t_sq2, tok_ln2free])
                        sch.run(TE, lambda En, fc=fc: En.matmul(
                            PSB[:, 0:512], onesf[:, :], r2_t[fc][:, :],
                            start=(fc == 0), stop=(fc == NC - 1)))
                        t_ln2 = sch.run(TE, lambda En, fc=fc: En.matmul(
                            PSB[:, 512:1024], onesb[:, :], sq2_t[fc % 2][:, :],
                            start=(fc == 0), stop=(fc == NC - 1)), inc=1)
                        t_ln2h[ln2_i] = t_ln2
                        ln2_i += 1
                sch.waits(VE, [t_ln2])
                t_m2 = sch.run(VE, lambda En: En.tensor_scalar(
                    m_t2[:, :], PSB[:, 0:512], 1.0 / D, None, op0=ALU.mult), inc=1)
                sch.waits(SC, [t_m2])
                t_msq2 = sch.run(SC, lambda En: En.activation(msq2[:, :], m_t2[:, :], ACT.Square), inc=1)
                sch.waits(VE, [t_msq2])
                t_var2 = sch.run(VE, lambda En: En.scalar_tensor_tensor(
                    var2[:, :], PSB[:, 512:1024], 1.0 / D, msq2[:, :],
                    op0=ALU.mult, op1=ALU.subtract), inc=1)
                tok_ln2free = t_var2
                sch.waits(SC, [t_var2])
                t_srt2 = sch.run(SC, lambda En: En.activation(
                    srt2[:, :], var2[:, :], ACT.Sqrt, bias=epsln[:, 0:1]), inc=1)
                sch.waits(VE, [t_srt2])
                sch.run(VE, lambda En: En.reciprocal(rstd2[:, :], srt2[:, :]))
                sch.run(VE, lambda En: En.memset(junkg[:, 0:1], 0.0))
                sch.run(VE, lambda En: En.tensor_tensor(bb2[:, :], m_t2[:, :], rstd2[:, :], op=ALU.mult))
                for fc in range(NC):
                    pf = fc % 2
                    sch.wait(VE, tok_yst[pf])
                    sch.run(VE, lambda En, fc=fc, pf=pf: En.tensor_tensor(
                        t1_t2[pf][:, :], r2_t[fc][:, :], rstd2[:, :], op=ALU.mult))
                    t_y = sch.run(VE, lambda En, fc=fc, pf=pf: En.tensor_tensor(
                        y_t[pf][:, :], t1_t2[pf][:, :], bb2[:, :], op=ALU.subtract), inc=1)
                    sch.waits(SY, [t_y])
                    tok_yst[pf] = sch.dma("yst", lambda En, fc=fc, sl=sl, pf=pf, s=s: En.dma_start(
                        out=yT[s, fc * 128:(fc + 1) * 128, sl * 512:(sl + 1) * 512],
                        in_=y_t[pf][:, :]))

            sch.barrier()
            mem.reset(seq_mark)

        sch.waits("sync", [(st, sch.cnt[st]) for st in sch.streams])

        blk = ctx.enter_context(nc.Block())

        @blk.sync
        def _(En):
            for th in sch.ops["sync"]:
                th(En)

        @blk.tensor
        def _(En):
            for th in sch.ops["tensor"]:
                th(En)

        @blk.scalar
        def _(En):
            for th in sch.ops["scalar"]:
                th(En)

        @blk.vector
        def _(En):
            for th in sch.ops["vector"]:
                th(En)

    return nc


def make_shared_inputs(qkv_w, qkv_b, out_w, out_b, gate_w, gate_b, exp_w, exp_b):
    order = _build_order()
    qkvbP = qkv_b[order]
    COS, SIN, P, OQK, IDENT, E8 = _build_consts()
    return {
        "qkvwT": np.ascontiguousarray(qkv_w.T[:, order]),
        "qkvbt_d": np.ascontiguousarray(qkvbP[0:2 * D].reshape(16, 128).T),
        "vb_d": np.ascontiguousarray(qkvbP[2 * D:3 * D].reshape(1, D)),
        "outwT": np.ascontiguousarray(out_w.T),
        "outbt_d": np.ascontiguousarray(out_b.reshape(8, 128).T),
        "gatew_d": np.ascontiguousarray(gate_w.reshape(NC, 128, E).transpose(1, 0, 2)),
        "gateb": np.ascontiguousarray(gate_b.reshape(1, E)),
        "expw": exp_w, "expb": exp_b,
        "cosd": COS, "sind": SIN, "pmatd": P, "oqkd": OQK, "identd": IDENT,
        "e8d": E8,
    }


def kernel(L0_attn_tokens, L0_attn_qkv_w, L0_attn_qkv_b, L0_attn_out_w,
           L0_attn_out_b, L0_ffn_gate_w, L0_ffn_gate_b, L0_ffn_expert_w,
           L0_ffn_expert_b):
    toks = np.asarray(L0_attn_tokens, np.float32)
    shared = make_shared_inputs(
        np.asarray(L0_attn_qkv_w, np.float32), np.asarray(L0_attn_qkv_b, np.float32),
        np.asarray(L0_attn_out_w, np.float32), np.asarray(L0_attn_out_b, np.float32),
        np.asarray(L0_ffn_gate_w, np.float32), np.asarray(L0_ffn_gate_b, np.float32),
        np.asarray(L0_ffn_expert_w, np.float32), np.asarray(L0_ffn_expert_b, np.float32))

    nc = build_program()

    in_maps = []
    for c in range(NCORES):
        m = dict(shared)
        m["xT"] = np.ascontiguousarray(toks[2 * c:2 * c + 2].transpose(0, 2, 1))
        in_maps.append(m)

    res = run_bass_kernel_spmd(nc, in_maps, core_ids=list(range(NCORES)))

    out = np.empty((16, S, D), dtype=np.float32)
    for c in range(NCORES):
        yTc = res.results[c]["yT"]
        out[2 * c] = yTc[0].T
        out[2 * c + 1] = yTc[1].T
    return out
